# version 01
# baseline (speedup 1.0000x reference)
import numpy as np
import ml_dtypes
from contextlib import ExitStack
import concourse.bass as bass
import concourse.mybir as mybir
from concourse.bass_utils import run_bass_kernel_spmd

F32 = mybir.dt.float32
BF16 = mybir.dt.bfloat16
I32 = mybir.dt.int32
AF = mybir.ActivationFunctionType
ALU = mybir.AluOpType
BF = ml_dtypes.bfloat16

NCORES = 8
B, S, D = 2, 4096, 4096
T_ALL = B * S
T_C = T_ALL // NCORES
HEADS, HD, KVH = 32, 128, 8
H_C = HEADS // NCORES
NEG_BIG = -1e9

ENGS = ("te", "ve", "sc", "gp", "sy")


class Plan:

    def __init__(self, nc, stack):
        self.nc = nc
        self.stack = stack
        self.evs = {e: [] for e in ENGS}
        self.sems = {}
        self.semc = {}

    def sem(self, name):
        h = self.stack.enter_context(self.nc.semaphore(name))
        self.sems[name] = h
        self.semc[name] = 0
        return h

    def emit(self, eng, fn):
        self.evs[eng].append(fn)

    def bump(self, sem, k):
        self.semc[sem] += k
        return self.semc[sem]

    def cnt(self, sem):
        return self.semc[sem]

    def wait(self, eng, sem, val=None):
        v = self.semc[sem] if val is None else val
        h = self.sems[sem]
        self.emit(eng, lambda e, h=h, v=v: e.wait_ge(h, v))

    def op(self, eng, meth, *args, sem=None, k=1, **kw):
        h = self.sems[sem] if sem else None
        c = self.bump(sem, k) if sem else None

        def fn(e, meth=meth, args=args, kw=kw, h=h, k=k):
            ins = getattr(e, meth)(*args, **kw)
            if h is not None:
                ins.then_inc(h, k)
        self.emit(eng, fn)
        return c

    def dma(self, eng, out, in_, sem, **kw):
        return self.op(eng, "dma_start", sem=sem, k=16, out=out, in_=in_, **kw)

    def build(self):
        block = self.stack.enter_context(self.nc.Block())
        evs = self.evs

        def run(e, lst):
            for fn in lst:
                fn(e)
        block.tensor(lambda e: run(e, evs["te"]))
        block.vector(lambda e: run(e, evs["ve"]))
        block.scalar(lambda e: run(e, evs["sc"]))
        block.gpsimd(lambda e: run(e, evs["gp"]))
        block.sync(lambda e: run(e, evs["sy"]))


def transpose_in(p, src, dst_fn, ntiles, trp, eye, tr_sem, ev_sem, hist,
                 dtype_note=None, evac_eng="ve", evac_meth="tensor_copy"):
    for g in range(0, ntiles, 4):
        n = min(4, ntiles - g)
        idx = hist["n"]
        tp = trp[idx % 2]
        if idx >= 2:
            p.wait("te", ev_sem, hist["ev"][idx - 2])
        for j in range(n):
            p.op("te", "transpose", tp[:, j * 128:(j + 1) * 128],
                 src[:, (g + j) * 128:(g + j + 1) * 128], eye,
                 sem=(tr_sem if j == n - 1 else None))
        p.wait(evac_eng, tr_sem)
        for j in range(n):
            p.op(evac_eng, evac_meth, dst_fn(g + j), tp[:, j * 128:(j + 1) * 128],
                 sem=(ev_sem if j == n - 1 else None))
        hist["ev"][idx] = p.cnt(ev_sem)
        hist["n"] += 1


def build_launch1():
    nc = bass.Bass("TRN2", target_bir_lowering=False, debug=False)
    stack = ExitStack()
    p = Plan(nc, stack)

    x_d = nc.dram_tensor("x", [T_C, D], F32, kind="ExternalInput").ap()
    wkr_d = nc.dram_tensor("wkr", [512, D], F32, kind="ExternalInput").ap()
    wdk_d = nc.dram_tensor("wdk", [512, D], F32, kind="ExternalInput").ap()
    eye_d = nc.dram_tensor("eye", [128, 128], F32, kind="ExternalInput").ap()
    xT_o = nc.dram_tensor("xT", [D, T_C], BF16, kind="ExternalOutput").ap()
    krT_o = nc.dram_tensor("krT", [512, T_C], BF16, kind="ExternalOutput").ap()
    ckvT_o = nc.dram_tensor("ckvT", [512, T_C], BF16, kind="ExternalOutput").ap()

    sb = lambda n, s, t: stack.enter_context(nc.sbuf_tensor(n, s, t))
    ps = lambda n, s, t: stack.enter_context(nc.psum_tensor(n, s, t))

    eye = sb("eye_sb", [128, 128], BF16)
    xtok = [sb(f"xtok{i}", [128, D], BF16) for i in range(2)]
    xT = sb("xTs", [128, 32 * T_C], BF16)
    nat = [sb(f"nat{i}", [128, D], BF16) for i in range(2)]
    wTs = sb("wTs", [128, 32 * 128], BF16)
    outT = sb("outT", [128, 1024], BF16)
    trp = [ps(f"trp{i}", [128, 512], BF16) for i in range(2)]
    acc = [ps(f"acc{i}", [128, 512], F32) for i in range(2)]

    for s_ in ("ld", "w_ld", "w_tr", "w_ev", "mm", "ev", "st"):
        p.sem(s_)
    hist = {"n": 0, "ev": {}}

    p.dma("gp", eye[:], eye_d[:], "ld")
    p.wait("te", "ld", 16)

    xld = {}
    for tt in range(8):
        xb = xtok[tt % 2]
        if tt >= 2:
            p.wait("gp", "w_tr", p.cnt("w_tr"))
        xld[tt] = p.dma("gp", xb[:], x_d[tt * 128:(tt + 1) * 128, :], "ld")
        p.wait("te", "ld", xld[tt])
        transpose_in(
            p, xb,
            lambda j, tt=tt: xT[:, j * 1024 + tt * 128: j * 1024 + (tt + 1) * 128],
            32, trp, eye[:], "w_tr", "w_ev", hist)
    p.wait("sy", "w_ev", p.cnt("w_ev"))
    p.dma("sy", xT_o.rearrange("(f p) t -> p f t", p=128),
          xT[:].rearrange("p (f t) -> p f t", t=T_C), "st")

    def proj(w_d, out_o):
        for fo in range(4):
            nb = nat[fo % 2]
            p.wait("gp", "w_tr", p.cnt("w_tr"))
            c = p.dma("gp", nb[:], w_d[fo * 128:(fo + 1) * 128, :], "w_ld")
            p.wait("te", "w_ld", c)
            p.wait("te", "mm", p.cnt("mm"))
            transpose_in(p, nb, lambda j: wTs[:, j * 128:(j + 1) * 128],
                         32, trp, eye[:], "w_tr", "w_ev", hist)
            p.wait("te", "w_ev", p.cnt("w_ev"))
            p.wait("te", "ev", p.cnt("ev"))
            for fi in range(32):
                for ch in range(2):
                    p.op("te", "matmul", acc[ch][:],
                         wTs[:, fi * 128:(fi + 1) * 128],
                         xT[:, fi * 1024 + ch * 512: fi * 1024 + (ch + 1) * 512],
                         start=(fi == 0), stop=(fi == 31),
                         sem=("mm" if (fi == 31 and ch == 1) else None))
            p.wait("ve", "mm", p.cnt("mm"))
            p.wait("ve", "st", p.cnt("st"))
            p.op("ve", "tensor_copy", outT[:, 0:512], acc[0][:])
            p.op("ve", "tensor_copy", outT[:, 512:1024], acc[1][:], sem="ev")
            p.wait("sy", "ev", p.cnt("ev"))
            p.dma("sy", out_o[fo * 128:(fo + 1) * 128, :], outT[:], "st")

    proj(wkr_d, krT_o)
    proj(wdk_d, ckvT_o)
    p.wait("sy", "st", p.cnt("st"))
    p.build()
    stack.close()
    return nc


def _f32(x):
    return np.ascontiguousarray(np.asarray(x, dtype=np.float32))


_cache = {}


def kernel(**inputs):
    raise NotImplementedError("under construction")


def build_launch2(dbg=False):
    nc = bass.Bass("TRN2", target_bir_lowering=False, debug=False)
    stack = ExitStack()
    p = Plan(nc, stack)
    SQ = 1.0 / float(np.sqrt(128.0))

    xT_d = nc.dram_tensor("xT", [D, T_ALL], BF16, kind="ExternalInput").ap()
    kr_d = nc.dram_tensor("krTm", [64, T_ALL], BF16, kind="ExternalInput").ap()
    ckv_d = nc.dram_tensor("ckvT", [512, T_ALL], BF16, kind="ExternalInput").ap()
    wq_d = nc.dram_tensor("wq", [512, D], F32, kind="ExternalInput").ap()
    wupk_d = nc.dram_tensor("wupk", [64, 512], F32, kind="ExternalInput").ap()
    wupv_d = nc.dram_tensor("wupv", [128, 512], F32, kind="ExternalInput").ap()
    cosL_d = nc.dram_tensor("cosL", [64, S], F32, kind="ExternalInput").ap()
    cosH_d = nc.dram_tensor("cosH", [64, S], F32, kind="ExternalInput").ap()
    sinL_d = nc.dram_tensor("sinL", [64, S], F32, kind="ExternalInput").ap()
    sinH_d = nc.dram_tensor("sinH", [64, S], F32, kind="ExternalInput").ap()
    tri_d = nc.dram_tensor("tri", [128, 4 * 512], F32, kind="ExternalInput").ap()
    eye_d = nc.dram_tensor("eye", [128, 128], F32, kind="ExternalInput").ap()
    qT_d = nc.dram_tensor("qTd", [512, T_ALL], BF16,
                          kind=("ExternalOutput" if dbg else "Internal")).ap()
    attnT_o = nc.dram_tensor("attnT", [512, T_ALL], BF16, kind="ExternalOutput").ap()
    if dbg:
        kT_o = nc.dram_tensor("kTo", [128, T_ALL], BF16, kind="ExternalOutput").ap()
        va_o = nc.dram_tensor("vao", [128, 64 * 129], BF16, kind="ExternalOutput").ap()
        nm_o = nc.dram_tensor("nmo", [128, 64], F32, kind="ExternalOutput").ap()
        qn_o = nc.dram_tensor("qno", [1, 80 * 16], F32, kind="ExternalOutput").ap()

    sb = lambda n, s, t: stack.enter_context(nc.sbuf_tensor(n, s, t))
    ps = lambda n, s, t: stack.enter_context(nc.psum_tensor(n, s, t))

    eye = sb("eye_sb", [128, 128], BF16)
    eyef = sb("eyef_sb", [128, 128], F32)
    tri = sb("tri_sb", [128, 4 * 512], BF16)
    wqT = sb("wqT", [128, 32 * 512], BF16)
    pan = [sb(f"pan{i}", [128, 32 * 512], BF16) for i in range(2)]
    cosL = sb("cosL_sb", [64, S], BF16)
    cosH = sb("cosH_sb", [64, S], BF16)
    sinL = sb("sinL_sb", [64, S], BF16)
    sinH = sb("sinH_sb", [64, S], BF16)
    tmp = sb("tmp", [64, 4 * 512], F32)
    ktmp = sb("ktmp2", [32, 6 * 512], BF16)
    qstg = [sb(f"qstg{i}", [128, 512], BF16) for i in range(2)]
    kT = sb("kT", [128, T_ALL], BF16)
    knhi = [sb(f"knhi{i}", [32, 512], BF16) for i in range(2)]
    wupkT = sb("wupkT", [128, 4 * 64], BF16)
    wupvT = sb("wupvT", [128, 4 * 128], BF16)
    cpan = [sb(f"cpan{i}", [128, 4 * 512], BF16) for i in range(2)]
    vaug = sb("vaug", [128, 64 * 129], BF16)
    qwin = [sb(f"qwin{i}", [128, 512], BF16) for i in range(2)]
    pT = [sb(f"pT{i}", [128, 512], BF16) for i in range(2)]
    rd = sb("rd", [128, 4 * 16], F32)
    astg = [sb(f"astg{i}", [128, 128], BF16) for i in range(2)]
    ostg = sb("ostg", [128, 512], BF16)
    onesr = sb("onesr", [1, 128], BF16)
    onesc = sb("onesc", [128, 1], BF16)
    negm = sb("negm", [128, 4 * 16], F32)
    sqstg = [sb(f"sqstg{i}", [128, 512], BF16) for i in range(2)]
    qn2 = sb("qn2", [1, 80 * 16], F32)
    sc1 = sb("sc1", [1, 64], F32)
    sc2 = sb("sc2", [1, 64], BF16)

    A = [ps(f"A{i}", [128, 512], F32) for i in range(2)]
    PV = [ps(f"PV{i}", [128, 512], F32) for i in range(4)]
    QA = PV[0:2]
    T0 = [ps(f"T{i}", [128, 512], BF16) for i in range(2)]

    for s_ in ("ld", "w_ld", "w_tr", "w_ev", "mm", "ev", "st",
               "t1", "t2", "t3", "pv", "ex", "qd"):
        p.sem(s_)
    hist = {"n": 0, "ev": {}}

    p.dma("gp", eye[:], eye_d[:], "ld")
    p.dma("gp", tri[:], tri_d[:], "ld")
    p.dma("gp", cosL[:], cosL_d[:], "ld")
    p.dma("gp", cosH[:], cosH_d[:], "ld")
    p.dma("gp", sinL[:], sinL_d[:], "ld")
    p.dma("gp", sinH[:], sinH_d[:], "ld")
    p.dma("sy", eyef[:], eye_d[:], "ld")
    ld_const = p.cnt("ld")
    p.wait("ve", "ld", ld_const)
    p.op("ve", "memset", onesr[:], 1.0)
    p.op("ve", "memset", onesc[:], 1.0)
    p.op("ve", "memset", vaug[:], 1.0, sem="ev")
    vaug_ones = p.cnt("ev")

    p.wait("te", "ld", ld_const)
    for fs in range(4):
        nb = pan[0]
        c = p.dma("gp", nb[:, 0:4096], wq_d[fs * 128:(fs + 1) * 128, :], "w_ld")
        p.wait("te", "w_ld", c)
        transpose_in(p, nb,
                     lambda j, fs=fs: wqT[:, j * 512 + fs * 128: j * 512 + (fs + 1) * 128],
                     32, T0, eye[:], "w_tr", "w_ev", hist)
        p.wait("gp", "w_tr", p.cnt("w_tr"))
    wq_ready = p.cnt("w_ev")

    mm_panel = {}
    for tb in range(16):
        pb = pan[tb % 2]
        b = tb // 8
        if tb == 0:
            p.wait("sy", "w_tr", p.cnt("w_tr"))
        if tb >= 2:
            p.wait("sy", "mm", mm_panel[tb - 2])
        c = p.dma("sy", pb[:].rearrange("p (f t) -> p f t", t=512),
                  xT_d.rearrange("(f p) t -> p f t", p=128)[:, :, tb * 512:(tb + 1) * 512],
                  "ld")
        p.wait("te", "ld", c)
        if tb == 0:
            p.wait("te", "w_ev", wq_ready)
        for half in range(2):
            for i2 in range(2):
                fo = half * 2 + i2
                p.wait("te", "ev", p.cnt("ev"))
                for fi in range(32):
                    p.op("te", "matmul", QA[i2][:],
                         wqT[:, fi * 512 + fo * 128: fi * 512 + (fo + 1) * 128],
                         pb[:, fi * 512:(fi + 1) * 512],
                         start=(fi == 0), stop=(fi == 31),
                         sem=("mm" if fi == 31 else None))
            mm_half = p.cnt("mm")
            p.wait("ve", "mm", mm_half)
            for i2 in range(2):
                fo = half * 2 + i2
                h = fo
                qa = QA[i2]
                stg = qstg[i2]
                cs = (tb % 8) * 512
                coL, coH = cosL[:, cs:cs + 512], cosH[:, cs:cs + 512]
                siL, siH = sinL[:, cs:cs + 512], sinH[:, cs:cs + 512]
                tA = tmp[:, 0:512]
                tB = tmp[:, 512:1024]
                tC = tmp[:, 1024:1536]
                tD = tmp[:, 1536:2048]
                p.wait("ve", "qd", p.cnt("qd"))
                p.wait("ve", "t1", p.cnt("t1"))
                p.op("ve", "tensor_mul", tA[:], qa[64:128, :], coH)
                p.op("ve", "tensor_mul", tB[:], qa[0:64, :], siH)
                p.op("ve", "tensor_mul", tC[:], qa[64:128, :], siL)
                p.op("ve", "tensor_mul", tD[:], qa[0:64, :], coL)
                p.op("ve", "tensor_add", stg[64:128, :], tA[:], tB[:])
                p.op("ve", "tensor_sub", stg[0:64, :], tD[:], tC[:], sem="ev")
                rope_ev = p.cnt("ev")
                p.wait("sc", "ev", rope_ev)
                p.wait("sc", "t2", p.cnt("t2"))
                p.op("sc", "activation", sqstg[i2][:], stg[:], AF.Square,
                     sem="t1")
                p.wait("te", "t1", p.cnt("t1"))
                p.wait("te", "t3", p.cnt("t3"))
                p.op("te", "matmul", A[i2][0:1, :], onesc[:], sqstg[i2][:],
                     start=True, stop=True, sem="t2")
                p.wait("ve", "t2", p.cnt("t2"))
                p.op("ve", "reduce_max", qn2[:, (h * 16 + tb) * 16:(h * 16 + tb) * 16 + 1],
                     A[i2][0:1, :], mybir.AxisListType.X, sem="t3")
                p.wait("sy", "ev", rope_ev)
                p.dma("sy", qT_d[h * 128:(h + 1) * 128, tb * 512:(tb + 1) * 512],
                      stg[:], "qd")
        mm_panel[tb] = p.cnt("mm")
    qproj_done = p.cnt("qd")
    qn2_done = p.cnt("t3")

    p.dma("sy", kT[0:32, :], kr_d[0:32, :], "ld")
    c = p.dma("sy", kT[64:96, :], kr_d[32:64, :], "ld")
    p.wait("ve", "ld", c)
    for b in range(B):
        for chk in range(8):
            sl = slice(b * S + chk * 512, b * S + chk * 512 + 512)
            cs = slice(chk * 512, chk * 512 + 512)
            ta = ktmp[:, 0:512]
            tb2 = ktmp[:, 512:1024]
            t1 = ktmp[:, 1024:1536]
            t2 = ktmp[:, 1536:2048]
            t3 = ktmp[:, 2048:2560]
            t4 = ktmp[:, 2560:3072]
            p.op("ve", "tensor_copy", ta[:], kT[0:32, sl])
            p.op("ve", "tensor_copy", tb2[:], kT[64:96, sl])
            p.op("ve", "tensor_mul", t1[:], tb2[:], cosH[0:32, cs])
            p.op("ve", "tensor_mul", t2[:], ta[:], sinH[0:32, cs])
            p.op("ve", "tensor_mul", t3[:], tb2[:], sinL[0:32, cs])
            p.op("ve", "tensor_mul", t4[:], ta[:], cosL[0:32, cs])
            p.op("ve", "tensor_add", kT[64:96, sl], t1[:], t2[:])
            p.op("ve", "tensor_sub", kT[0:32, sl], t4[:], t3[:],
                 sem=("ev" if (b == B - 1 and chk == 7) else None))
    krope_done = p.cnt("ev")

    p.wait("gp", "mm", mm_panel[15])
    c = p.dma("gp", pan[0][0:64, 0:512], wupk_d[:], "w_ld")
    c = p.dma("gp", pan[0][0:128, 512:1024], wupv_d[:], "w_ld")
    p.wait("te", "w_ld", c)
    p.wait("te", "w_ev", wq_ready)
    for j in range(4):
        p.op("te", "transpose", T0[0][:, j * 64:(j + 1) * 64],
             pan[0][0:64, j * 128:(j + 1) * 128], eye[0:64, 0:64],
             sem=("w_tr" if j == 3 else None))
    p.wait("ve", "w_tr", p.cnt("w_tr"))
    p.op("ve", "tensor_copy", wupkT[:], T0[0][:, 0:256], sem="w_ev")
    for j in range(4):
        p.op("te", "transpose", T0[1][:, j * 128:(j + 1) * 128],
             pan[0][0:128, 512 + j * 128: 512 + (j + 1) * 128], eye[:],
             sem=("w_tr" if j == 3 else None))
    p.wait("ve", "w_tr", p.cnt("w_tr"))
    p.op("ve", "tensor_copy", wupvT[:], T0[1][:, 0:512], sem="w_ev")
    upw_done = p.cnt("w_ev")
    p.wait("te", "w_ev", upw_done)

    p.wait("te", "ev", vaug_ones)
    kn_mm = {}
    kn_ev = {}
    knhi_st = {}
    for ch in range(16):
        cb = cpan[ch % 2]
        if ch >= 2:
            p.wait("sy", "mm", kn_mm[ch - 2])
        c = p.dma("sy", cb[:].rearrange("p (f t) -> p f t", t=512),
                  ckv_d.rearrange("(f p) t -> p f t", p=128)[:, :, ch * 512:(ch + 1) * 512],
                  "ld")
        p.wait("te", "ld", c)
        if ch >= 1:
            p.wait("te", "ev", kn_ev[ch - 1])
        for fi in range(4):
            p.op("te", "matmul", A[0][0:64, :],
                 wupkT[:, fi * 64:(fi + 1) * 64],
                 cb[:, fi * 512:(fi + 1) * 512],
                 start=(fi == 0), stop=(fi == 3))
        for tt in range(4):
            for fi in range(4):
                p.op("te", "matmul", A[1][:, tt * 128:(tt + 1) * 128],
                     cb[:, fi * 512 + tt * 128: fi * 512 + (tt + 1) * 128],
                     wupvT[:, fi * 128:(fi + 1) * 128],
                     start=(fi == 0), stop=(fi == 3),
                     sem=("mm" if (tt == 3 and fi == 3) else None))
        kn_mm[ch] = p.cnt("mm")
        p.wait("ve", "mm", kn_mm[ch])
        if ch >= 2:
            p.wait("ve", "ld", knhi_st[ch - 2])
        p.op("ve", "tensor_copy", kT[32:64, ch * 512:(ch + 1) * 512], A[0][0:32, :])
        p.op("ve", "tensor_copy", knhi[ch % 2][:], A[0][32:64, :])
        for tt in range(4):
            gt = ch * 4 + tt
            p.op("ve", "tensor_copy", vaug[:, gt * 129: gt * 129 + 128],
                 A[1][:, tt * 128:(tt + 1) * 128],
                 sem=("ev" if tt == 3 else None))
        kn_ev[ch] = p.cnt("ev")
        p.wait("sy", "ev", kn_ev[ch])
        knhi_st[ch] = p.dma("sy", kT[96:128, ch * 512:(ch + 1) * 512],
                            knhi[ch % 2][:], "ld")
    kv_done = p.cnt("ld")
    kv_done_ev = p.cnt("ev")

    p.wait("ve", "ev", krope_done)
    p.wait("ve", "ld", kv_done)
    p.wait("sc", "ev", krope_done)
    p.wait("sc", "ld", kv_done)
    for ch in range(16):
        i2 = ch % 2
        p.wait("sc", "t2", p.cnt("t2"))
        p.op("sc", "activation", sqstg[i2][:], kT[:, ch * 512:(ch + 1) * 512],
             AF.Square, sem="t1")
        p.wait("te", "t1", p.cnt("t1"))
        p.wait("te", "t3", p.cnt("t3"))
        p.op("te", "matmul", A[i2][0:1, :], onesc[:], sqstg[i2][:],
             start=True, stop=True, sem="t2")
        p.wait("ve", "t2", p.cnt("t2"))
        p.op("ve", "reduce_max", qn2[:, (64 + ch) * 16:(64 + ch) * 16 + 1],
             A[i2][0:1, :], mybir.AxisListType.X, sem="t3")
    kslots = qn2[0:1].rearrange("a (c s) -> a s c", s=16)[:, 0:1, 64:80]
    p.op("ve", "reduce_max", sc1[:, 16:17], kslots,
         mybir.AxisListType.X, sem="t1")
    p.wait("sc", "t1", p.cnt("t1"))
    for h in range(H_C):
        hslots = qn2[0:1].rearrange("a (c s) -> a s c", s=16)[:, 0:1, h * 16:(h + 1) * 16]
        p.wait("ve", "t3", qn2_done)
        p.op("ve", "reduce_max", sc1[:, 32:33], hslots,
             mybir.AxisListType.X, sem="t1")
        p.wait("sc", "t1", p.cnt("t1"))
        p.op("sc", "activation", sc1[:, 48:49], sc1[:, 32:33], AF.Copy,
             scale=sc1[:, 16:17], sem="t3")
        p.wait("ve", "t3", p.cnt("t3"))
        p.op("ve", "tensor_copy", sc1[:, 0:1], sc1[:, 48:49], sem="t1")
        p.wait("sc", "t1", p.cnt("t1"))
        p.op("sc", "activation", sc1[:, 8:9], sc1[:, 0:1], AF.Sqrt, sem="t3")
        p.wait("ve", "t3", p.cnt("t3"))
        p.op("ve", "tensor_single_scalar", sc2[:, 0:1], sc1[:, 8:9],
             -0.08838834764831845, ALU.mult, sem="t1")
        p.wait("te", "t1", p.cnt("t1"))
        p.op("te", "matmul", A[1][:, 0:1], onesr[:], sc2[:, 0:1],
             start=True, stop=True, sem="t2")
        p.wait("ve", "t2", p.cnt("t2"))
        p.op("ve", "tensor_copy", negm[:, h * 16:h * 16 + 1], A[1][:, 0:1],
             sem="t3")
    negm_done = p.cnt("t3")

    p.wait("sc", "t3", negm_done)
    p.wait("te", "t3", negm_done)
    p.wait("ve", "t3", negm_done)
    p.wait("sy", "t3", negm_done)
    exd = {}
    pbd = {0: 0, 1: 0}
    tbk = {0: 0, 1: 0}
    qw_mm = {}
    sctr = 0
    git = 0
    for h in range(H_C):
        for b in range(B):
            for qg in range(8):
                qw = qwin[git % 2]
                if git >= 2:
                    p.wait("sy", "mm", qw_mm[git - 2])
                c = p.dma("sy", qw[:],
                          qT_d[h * 128:(h + 1) * 128,
                               b * S + qg * 512: b * S + (qg + 1) * 512], "ld")
                p.wait("te", "ld", c)
                nkt = 4 * (qg + 1)
                for kt in range(nkt):
                    sbk = sctr % 2
                    sctr += 1
                    sp = A[sbk]
                    pb = pT[sbk]
                    if sbk in exd:
                        p.wait("te", "ex", exd[sbk])
                    p.op("te", "matmul", sp[:],
                         kT[:, b * S + kt * 128: b * S + (kt + 1) * 128],
                         qw[:], start=True, stop=True, sem="mm")
                    p.wait("sc", "mm", p.cnt("mm"))
                    p.wait("sc", "pv", pbd[sbk])
                    p.op("sc", "activation", pb[:], sp[:], AF.Exp,
                         bias=negm[:, h * 16:h * 16 + 1], scale=SQ, sem="ex")
                    exd[sbk] = p.cnt("ex")
                    if kt >= 4 * qg:
                        j = kt - 4 * qg
                        p.wait("ve", "ex", exd[sbk])
                        p.op("ve", "tensor_mul", pb[:], pb[:],
                             tri[:, j * 512:(j + 1) * 512], sem="ex")
                        exd[sbk] = p.cnt("ex")
                    p.wait("te", "ex", exd[sbk])
                    for su in range(4):
                        p.op("te", "matmul", PV[su][:, 0:129],
                             pb[:, su * 128:(su + 1) * 128],
                             vaug[:, (b * 32 + kt) * 129:(b * 32 + kt + 1) * 129],
                             start=(kt == 0), stop=(kt == nkt - 1),
                             sem=("pv" if su == 3 else None))
                    pbd[sbk] = p.cnt("pv")
                qw_mm[git] = p.cnt("mm")
                p.wait("ve", "pv", p.cnt("pv"))
                p.wait("ve", "t2", p.cnt("t2"))
                for su in range(4):
                    p.op("ve", "reciprocal", rd[:, su * 16:su * 16 + 1],
                         PV[su][:, 128:129], sem=("t1" if su == 3 else None))
                p.wait("sc", "t1", p.cnt("t1"))
                p.wait("sc", "w_tr", p.cnt("w_tr"))
                p.wait("ve", "st", p.cnt("st"))
                for su in range(4):
                    stg = astg[su % 2]
                    if su >= 2:
                        p.wait("sc", "w_tr", p.cnt("w_tr"))
                    p.op("sc", "activation", stg[:], PV[su][:, 0:128],
                         AF.Copy, scale=rd[:, su * 16:su * 16 + 1], sem="t2")
                    p.wait("te", "t2", p.cnt("t2"))
                    tb_ = su % 2
                    p.wait("te", "ev", tbk[tb_])
                    p.op("te", "transpose", T0[tb_][:, 0:128], stg[:], eye[:],
                         sem="w_tr")
                    p.wait("ve", "w_tr", p.cnt("w_tr"))
                    p.op("ve", "tensor_copy", ostg[:, su * 128:(su + 1) * 128],
                         T0[tb_][:, 0:128], sem="ev")
                    tbk[tb_] = p.cnt("ev")
                p.wait("te", "t2", p.cnt("t2"))
                p.wait("sy", "ev", p.cnt("ev"))
                p.dma("sy",
                      attnT_o[h * 128:(h + 1) * 128,
                              b * S + qg * 512: b * S + qg * 512 + 512],
                      ostg[:], "st")
                git += 1
    if dbg:
        p.dma("sy", kT_o[:], kT[:], "st")
        p.dma("sy", va_o[:], vaug[:], "st")
        p.dma("sy", nm_o[:], negm[:], "st")
        p.dma("sy", qn_o[:], qn2[:], "st")
    p.wait("sy", "st", p.cnt("st"))
    p.build()
    stack.close()
    return nc


def build_launch3(dbg=False):
    nc = bass.Bass("TRN2", target_bir_lowering=False, debug=False)
    stack = ExitStack()
    p = Plan(nc, stack)

    at_d = nc.dram_tensor("attnTm", [D, T_C], BF16, kind="ExternalInput").ap()
    x_d = nc.dram_tensor("x", [T_C, D], F32, kind="ExternalInput").ap()
    wo_d = nc.dram_tensor("wo", [D, D], F32, kind="ExternalInput").ap()
    fw_d = nc.dram_tensor("fw", [D, D], F32, kind="ExternalInput").ap()
    gw_d = nc.dram_tensor("gw", [8, D], F32, kind="ExternalInput").ap()
    ics_d = nc.dram_tensor("ics", [8, D], F32, kind="ExternalInput").ap()
    ocs_d = nc.dram_tensor("ocs", [8, D], F32, kind="ExternalInput").ap()
    fb_d = nc.dram_tensor("fb", [D, 1], F32, kind="ExternalInput").ap()
    eye_d = nc.dram_tensor("eye", [128, 128], F32, kind="ExternalInput").ap()
    x1_d = nc.dram_tensor("x1d", [T_C, D], F32,
                          kind=("ExternalOutput" if dbg else "Internal")).ap()
    x2_o = nc.dram_tensor("x2", [T_C, D], F32, kind="ExternalOutput").ap()
    if dbg:
        rt_o = nc.dram_tensor("rto", [8, 1024], F32, kind="ExternalOutput").ap()
        ytk_o = nc.dram_tensor("ytko", [128, 8 * D], BF16, kind="ExternalOutput").ap()
    x2T_o = nc.dram_tensor("x2T", [D, T_C], BF16, kind="ExternalOutput").ap()

    sb = lambda n, s, t: stack.enter_context(nc.sbuf_tensor(n, s, t))
    ps = lambda n, s, t: stack.enter_context(nc.psum_tensor(n, s, t))

    eye = sb("eye_sb", [128, 128], BF16)
    eyef = sb("eyef_sb", [8, 8], F32)
    bufA = sb("bufA", [128, 32 * T_C], BF16)
    bufB = sb("bufB", [128, 8 * D], BF16)
    nat_ = sb("nat0", [128, D], BF16)
    nat = [nat_, nat_]
    wTs_ = sb("wTs0", [128, 32 * 128], BF16)
    wTs = [wTs_, wTs_]
    ystg = sb("ystg", [128, 1024], BF16)
    rstg = sb("rstg", [128, D], F32)
    astg = rstg
    bstg = sb("bstg", [128, D], BF16)
    junk = bstg
    gwT = sb("gwT", [128, 32 * 8], BF16)
    icst = sb("icst", [8, D], BF16)
    ocst = sb("ocst", [8, D], BF16)
    fbc = sb("fbc", [128, 32 * 16], F32)
    rt = sb("rt", [128, 8], F32)
    rtb = sb("rtb", [128, 8], BF16)
    routT = sb("routT", [8, 1024], BF16)
    os_sb = sb("os_sb", [128, 1024], F32)
    y2s = rstg[:, 0:1024]
    lg = os_sb[0:8, :]
    stt = sb("stt", [128, 16 * 16], F32)
    trp = [ps(f"trp{i}", [128, 512], BF16) for i in range(2)]
    ACC = [ps(f"ACC{i}", [128, 512], F32) for i in range(2)]
    OS = ps("OS", [128, 1024], F32)
    M8 = ps("M8", [128, 1024], F32)

    for s_ in ("ld", "w_ld", "w_tr", "w_ev", "mm", "ev", "st",
               "t1", "t2", "t3", "e1", "e2"):
        p.sem(s_)
    hist = {"n": 0, "ev": {}}

    p.op("ve", "memset", stt[:, 160:161], 1e-5)
    p.dma("gp", eye[:], eye_d[:], "ld")
    p.dma("sy", eyef[:], eye_d[0:8, 0:8], "ld")
    p.dma("gp", icst[:], ics_d[:], "ld")
    p.dma("gp", ocst[:], ocs_d[:], "ld")
    p.dma("sy", fbc[:].rearrange("p (a s) -> p a s", s=16)[:, :, 0:1],
          fb_d.rearrange("(a p) b -> p a b", p=128), "ld",
          allow_slow_non_contiguous=True)
    c = p.dma("sy", bufA[:].rearrange("p (f t) -> p f t", t=T_C),
              at_d.rearrange("(f p) t -> p f t", p=128), "ld")
    ld0 = p.cnt("ld")

    p.wait("te", "ld", ld0)
    mm_fo = {}
    for fo in range(32):
        nb = nat[fo % 2]
        ws = wTs[fo % 2]
        p.wait("gp", "w_tr", p.cnt("w_tr"))
        cw = p.dma("gp", nb[:], wo_d[fo * 128:(fo + 1) * 128, :], "w_ld")
        p.wait("te", "w_ld", cw)
        if fo >= 1:
            p.wait("te", "mm", mm_fo[fo - 1])
        transpose_in(p, nb, lambda j, ws=ws: ws[:, j * 128:(j + 1) * 128],
                     32, trp, eye[:], "w_tr", "w_ev", hist)
        if not hasattr(p, "trnat"):
            p.trnat = {}
        p.trnat[fo] = p.cnt("w_tr")
        p.wait("te", "w_ev", p.cnt("w_ev"))
        p.wait("te", "e1", p.cnt("e1"))
        for fi in range(32):
            for ch in range(2):
                p.op("te", "matmul", ACC[ch][:],
                     ws[:, fi * 128:(fi + 1) * 128],
                     bufA[:, fi * T_C + ch * 512: fi * T_C + (ch + 1) * 512],
                     start=(fi == 0), stop=(fi == 31),
                     sem=("mm" if (fi == 31 and ch == 1) else None))
        mm_fo[fo] = p.cnt("mm")
        p.wait("ve", "mm", mm_fo[fo])
        p.wait("ve", "w_tr", p.cnt("w_tr"))
        p.op("ve", "tensor_copy", ystg[:, 0:512], ACC[0][:])
        p.op("ve", "tensor_copy", ystg[:, 512:1024], ACC[1][:], sem="e1")
        p.wait("te", "e1", p.cnt("e1"))
        transpose_in(p, ystg,
                     lambda j, fo=fo: bufB[:, j * D + fo * 128: j * D + (fo + 1) * 128],
                     8, trp, eye[:], "w_tr", "w_ev", hist)
    y_done = p.cnt("w_ev")

    def ln_tile(tag):
        p.op("sc", "activation", junk[:], rstg[:], AF.Copy,
             accum_out=stt[:, 0:1])
        p.op("sc", "activation", junk[:], rstg[:], AF.Square,
             accum_out=stt[:, 16:17], sem="e2")
        p.wait("ve", "e2", p.cnt("e2"))
        p.op("ve", "tensor_single_scalar", stt[:, 32:33], stt[:, 0:1],
             1.0 / D, ALU.mult, sem="t1")
        p.wait("sc", "t1", p.cnt("t1"))
        p.op("sc", "activation", stt[:, 48:49], stt[:, 32:33], AF.Square,
             sem="e2")
        p.wait("ve", "e2", p.cnt("e2"))
        p.op("ve", "scalar_tensor_tensor", stt[:, 64:65], stt[:, 16:17],
             1.0 / D, stt[:, 48:49], ALU.mult, ALU.subtract, sem="t1")
        p.wait("sc", "t1", p.cnt("t1"))
        p.op("sc", "activation", stt[:, 80:81], stt[:, 64:65], AF.Sqrt,
             bias=stt[:, 160:161], sem="e2")
        p.wait("ve", "e2", p.cnt("e2"))
        p.op("ve", "reciprocal", stt[:, 96:97], stt[:, 80:81], sem="t1")
        p.wait("gp", "t1", p.cnt("t1"))
        p.op("gp", "tensor_scalar", astg[:], rstg[:], stt[:, 32:33],
             stt[:, 96:97], ALU.subtract, ALU.mult, sem="t2")
        p.wait("sc", "t2", p.cnt("t2"))
        p.op("sc", "copy", bstg[:], astg[:], sem="e2")

    p.wait("ve", "w_ev", y_done)
    for tt in range(8):
        p.wait("sy", "t2", p.cnt("t2"))
        p.wait("sy", "st", p.cnt("st"))
        cx = p.dma("sy", rstg[:], x_d[tt * 128:(tt + 1) * 128, :], "ld")
        p.wait("ve", "ld", cx)
        p.op("ve", "tensor_add", rstg[:], rstg[:],
             bufB[:, tt * D:(tt + 1) * D], sem="e1")
        p.wait("sc", "e1", p.cnt("e1"))
        ln_tile("q2")
        p.wait("sy", "t2", p.cnt("t2"))
        p.dma("sy", x1_d[tt * 128:(tt + 1) * 128, :], astg[:], "st")
        p.wait("te", "e2", p.cnt("e2"))
        transpose_in(p, bstg,
                     lambda j, tt=tt: bufA[:, j * T_C + tt * 128: j * T_C + (tt + 1) * 128],
                     32, trp, eye[:], "w_tr", "w_ev", hist)
        p.wait("sc", "w_tr", p.cnt("w_tr"))
        p.wait("sy", "e1", p.cnt("e1"))
    x1_done = p.cnt("w_ev")

    p.wait("te", "ld", ld0)
    p.wait("te", "w_ev", x1_done)
    cw = p.dma("gp", nat[0][0:8, :], gw_d[:], "w_ld")
    p.wait("te", "w_ld", cw)
    for j in range(32):
        p.op("te", "transpose", trp[j % 2][:, 0:8],
             nat[0][0:8, j * 128:(j + 1) * 128], eye[0:8, 0:8],
             sem="w_tr")
        p.wait("ve", "w_tr", p.cnt("w_tr"))
        p.op("ve", "tensor_copy", gwT[:, j * 8:(j + 1) * 8],
             trp[j % 2][:, 0:8], sem="w_ev")
        p.wait("te", "w_ev", p.cnt("w_ev"))
    for fi in range(32):
        for ch in range(2):
            p.op("te", "matmul", M8[0:8, ch * 512:(ch + 1) * 512],
                 gwT[:, fi * 8:(fi + 1) * 8],
                 bufA[:, fi * T_C + ch * 512: fi * T_C + (ch + 1) * 512],
                 start=(fi == 0), stop=(fi == 31),
                 sem=("mm" if (fi == 31 and ch == 1) else None))
    p.wait("ve", "mm", p.cnt("mm"))
    p.op("ve", "tensor_copy", lg[:], M8[0:8, 0:1024], sem="e1")
    for tt in range(8):
        p.wait("te", "e1", p.cnt("e1"))
        p.op("te", "transpose", M8[0:128, 0:8],
             lg[:, tt * 128:(tt + 1) * 128], eyef[:], sem="t2")
        p.wait("ve", "t2", p.cnt("t2"))
        p.op("ve", "reduce_max", stt[:, 112:113], M8[0:128, 0:8],
             mybir.AxisListType.X, negate=True, sem="t1")
        p.wait("sc", "t1", p.cnt("t1"))
        p.op("sc", "activation", rt[:], M8[0:128, 0:8], AF.Exp,
             bias=stt[:, 112:113], accum_out=stt[:, 128:129], sem="e2")
        p.wait("ve", "e2", p.cnt("e2"))
        p.op("ve", "reciprocal", stt[:, 144:145], stt[:, 128:129], sem="t1")
        p.wait("gp", "t1", p.cnt("t1"))
        p.op("gp", "tensor_scalar", rtb[:], rt[:], stt[:, 144:145], None,
             ALU.mult, sem="t3")
        p.wait("te", "t3", p.cnt("t3"))
        p.wait("te", "e1", p.cnt("e1"))
        p.op("te", "transpose", trp[tt % 2][0:8, 0:128], rtb[:], eye[:],
             sem="t2")
        p.wait("ve", "t2", p.cnt("t2"))
        p.op("ve", "tensor_copy", routT[:, tt * 128:(tt + 1) * 128],
             trp[tt % 2][0:8, 0:128], sem="e1")
    rout_done = p.cnt("e1")

    p.wait("te", "e1", rout_done)
    for fi in range(32):
        p.op("te", "matmul", M8[:, 0:512],
             icst[0:8, fi * 128:(fi + 1) * 128],
             routT[0:8, 0:512], start=True, stop=True)
        p.op("te", "matmul", M8[:, 512:1024],
             icst[0:8, fi * 128:(fi + 1) * 128],
             routT[0:8, 512:1024], start=True, stop=True, sem="t2")
        p.wait("ve", "t2", p.cnt("t2"))
        p.op("ve", "tensor_mul", bufA[:, fi * T_C:(fi + 1) * T_C],
             bufA[:, fi * T_C:(fi + 1) * T_C], M8[:, 0:1024], sem="t3")
        p.wait("te", "t3", p.cnt("t3"))
    mm_fo2 = {}
    for fo in range(32):
        nb = nat[fo % 2]
        ws = wTs[fo % 2]
        p.wait("gp", "w_tr", p.cnt("w_tr"))
        cw = p.dma("gp", nb[:], fw_d[fo * 128:(fo + 1) * 128, :], "w_ld")
        p.wait("te", "w_ld", cw)
        if fo >= 1:
            p.wait("te", "mm", mm_fo2[fo - 1])
        transpose_in(p, nb, lambda j, ws=ws: ws[:, j * 128:(j + 1) * 128],
                     32, trp, eye[:], "w_tr", "w_ev", hist,
                     evac_eng="sc", evac_meth="sign")
        p.trnat[1000 + fo] = p.cnt("w_tr")
        p.wait("te", "w_ev", p.cnt("w_ev"))
        p.wait("te", "e1", p.cnt("e1"))
        for ch in range(2):
            p.op("te", "matmul", OS[:, ch * 512:(ch + 1) * 512],
                 ocst[:, fo * 128:(fo + 1) * 128].rearrange("a b -> a b") if False
                 else ocst[0:8, fo * 128:(fo + 1) * 128],
                 routT[0:8, ch * 512:(ch + 1) * 512],
                 start=True, stop=True)
        for fi in range(32):
            for ch in range(2):
                p.op("te", "matmul", ACC[ch][:],
                     ws[:, fi * 128:(fi + 1) * 128],
                     bufA[:, fi * T_C + ch * 512: fi * T_C + (ch + 1) * 512],
                     start=(fi == 0), stop=(fi == 31),
                     sem=("mm" if (fi == 31 and ch == 1) else None))
        mm_fo2[fo] = p.cnt("mm")
        p.wait("ve", "mm", mm_fo2[fo])
        p.wait("ve", "w_tr", p.cnt("w_tr"))
        p.op("ve", "tensor_copy", os_sb[:], OS[:, 0:1024])
        p.op("ve", "tensor_mul", y2s[:, 0:512], ACC[0][:], os_sb[:, 0:512])
        p.op("ve", "tensor_mul", y2s[:, 512:1024], ACC[1][:], os_sb[:, 512:1024],
             sem="e1")
        p.wait("gp", "e1", p.cnt("e1"))
        p.wait("gp", "w_tr", p.cnt("w_tr"))
        p.op("gp", "tensor_scalar", ystg[:], y2s[:],
             fbc[:].rearrange("p (a s) -> p a s", s=16)[:, fo, 0:1], None,
             ALU.add, sem="t3")
        p.wait("te", "t3", p.cnt("t3"))
        transpose_in(p, ystg,
                     lambda j, fo=fo: bufB[:, j * D + fo * 128: j * D + (fo + 1) * 128],
                     8, trp, eye[:], "w_tr", "w_ev", hist)
    y2_done = p.cnt("w_ev")

    p.wait("ve", "w_ev", y2_done)
    p.wait("sy", "w_ev", y2_done)
    for tt in range(8):
        p.wait("sy", "t2", p.cnt("t2"))
        p.wait("sy", "st", p.cnt("st"))
        cx = p.dma("sy", rstg[:], x1_d[tt * 128:(tt + 1) * 128, :], "ld")
        p.wait("ve", "ld", cx)
        p.op("ve", "tensor_add", rstg[:], rstg[:],
             bufB[:, tt * D:(tt + 1) * D], sem="e1")
        p.wait("sc", "e1", p.cnt("e1"))
        ln_tile("q5")
        p.wait("sy", "t2", p.cnt("t2"))
        p.dma("sy", x2_o[tt * 128:(tt + 1) * 128, :], astg[:], "st")
        p.wait("te", "e2", p.cnt("e2"))
        transpose_in(p, bstg,
                     lambda j, tt=tt: bufA[:, j * T_C + tt * 128: j * T_C + (tt + 1) * 128],
                     32, trp, eye[:], "w_tr", "w_ev", hist)
        p.wait("sc", "w_tr", p.cnt("w_tr"))
        p.wait("sy", "e1", p.cnt("e1"))
    p.wait("sy", "w_ev", p.cnt("w_ev"))
    p.dma("sy", x2T_o.rearrange("(f p) t -> p f t", p=128),
          bufA[:].rearrange("p (f t) -> p f t", t=T_C), "st")
    if dbg:
        p.op("ve", "tensor_copy", os_sb[0:8, :], routT[:], sem="e1")
        p.wait("sy", "e1", p.cnt("e1"))
        p.dma("sy", rt_o[:], os_sb[0:8, :], "st")
        p.dma("sy", ytk_o[:], bufB[:], "st")
    p.wait("sy", "st", p.cnt("st"))
    p.build()
    stack.close()
    return nc


def build_launch4():
    nc = bass.Bass("TRN2", target_bir_lowering=False, debug=False)
    stack = ExitStack()
    p = Plan(nc, stack)

    xT_d = nc.dram_tensor("x2T", [D, T_ALL], BF16, kind="ExternalInput").ap()
    qw_d = nc.dram_tensor("qkvw", [1536, D], F32, kind="ExternalInput").ap()
    qdc_d = nc.dram_tensor("qdc", [128, 8], F32, kind="ExternalInput").ap()
    kdbd_d = nc.dram_tensor("kdbd", [128, 128], F32, kind="ExternalInput").ap()
    DT_d = nc.dram_tensor("DTt", [128, 4 * 2 * 256], F32, kind="ExternalInput").ap()
    eye_d = nc.dram_tensor("eye", [128, 128], F32, kind="ExternalInput").ap()
    qT_d = nc.dram_tensor("qTd", [512, T_ALL], BF16, kind="Internal").ap()
    kT_d = nc.dram_tensor("kTd", [512, T_ALL], BF16, kind="Internal").ap()
    v_d = nc.dram_tensor("vd", [H_C, 64, 128, 128], BF16, kind="Internal").ap()
    at_o = nc.dram_tensor("attn1T", [512, T_ALL], BF16, kind="ExternalOutput").ap()

    sb = lambda n, s, t: stack.enter_context(nc.sbuf_tensor(n, s, t))
    ps = lambda n, s, t: stack.enter_context(nc.psum_tensor(n, s, t))

    eye = sb("eye_sb", [128, 128], BF16)
    wT = sb("wT", [128, 32 * 1536], BF16)
    nat = sb("nat", [128, D], BF16)
    pan = [sb(f"pan{i}", [128, 32 * 512], BF16) for i in range(2)]
    sstg = sb("sstg", [128, 512], BF16)
    vstg = sb("vstg", [128, 512], BF16)
    qdc = sb("qdc_sb", [128, 8], F32)
    kdbd = sb("kdbd_sb", [128, 128], F32)
    DT = sb("DT_sb", [128, 2048], BF16)
    qwin = [sb(f"qwin{i}", [128, 512], BF16) for i in range(2)]
    kwin = sb("kwin", [128, 32 * 128], BF16)
    vwin = sb("vwin", [128, 32 * 128], BF16)
    pTi = [sb(f"pTi{i}", [128, 512], BF16) for i in range(2)]
    pTd = [sb(f"pTd{i}", [128, 512], BF16) for i in range(2)]
    isb = sb("isb", [128, 512], F32)
    astg = [sb(f"astg{i}", [128, 128], BF16) for i in range(2)]
    ostg = sb("ostg", [128, 512], BF16)

    SC = [ps(f"SC{i}", [128, 512], F32) for i in range(2)]
    IN = [ps(f"IN{i}", [128, 512], F32) for i in range(4)]
    IA = ps("IA", [128, 512], F32)
    TB = ps("TB", [128, 512], BF16)

    for s_ in ("ld", "w_ld", "w_tr", "w_ev", "mm", "ev", "st",
               "t1", "t2", "t3", "pv", "ex", "qd"):
        p.sem(s_)

    p.dma("gp", eye[:], eye_d[:], "ld")
    p.dma("gp", DT[:], DT_d[:], "ld")
    p.dma("sy", qdc[:], qdc_d[:], "ld")
    c0 = p.dma("sy", kdbd[:], kdbd_d[:], "ld")

    p.wait("te", "ld", c0)
    tr_ev = {}
    for fs in range(12):
        cw = p.dma("gp", nat[:], qw_d[fs * 128:(fs + 1) * 128, :], "w_ld")
        p.wait("te", "w_ld", cw)
        for g in range(0, 32, 4):
            gi = fs * 8 + g // 4
            if gi >= 1:
                p.wait("te", "w_ev", tr_ev[gi - 1])
            for j in range(4):
                fi = g + j
                p.op("te", "transpose", TB[:, j * 128:(j + 1) * 128],
                     nat[:, fi * 128:(fi + 1) * 128], eye[:],
                     sem=("w_tr" if j == 3 else None))
            p.wait("ve", "w_tr", p.cnt("w_tr"))
            for j in range(4):
                fi = g + j
                p.op("ve", "tensor_copy",
                     wT[:, fi * 1536 + fs * 128: fi * 1536 + (fs + 1) * 128],
                     TB[:, j * 128:(j + 1) * 128],
                     sem=("w_ev" if j == 3 else None))
            tr_ev[gi] = p.cnt("w_ev")
        p.wait("gp", "w_tr", p.cnt("w_tr"))
    w_ready = p.cnt("w_ev")

    p.wait("te", "w_ev", w_ready)
    ACC = [SC[0], SC[1], IN[0], IN[1], IN[2], IN[3]]
    mm_p = {}
    for tb in range(16):
        pb = pan[tb % 2]
        b = tb // 8
        if tb >= 2:
            p.wait("sy", "mm", mm_p[tb - 2])
        c = p.dma("sy", pb[:].rearrange("p (f t) -> p f t", t=512),
                  xT_d.rearrange("(f p) t -> p f t", p=128)[:, :, tb * 512:(tb + 1) * 512],
                  "ld")
        p.wait("te", "ld", c)
        for grp in range(2):
            p.wait("te", "ex", p.cnt("ex"))
            for f6 in range(6):
                fo = grp * 6 + f6
                for fi in range(32):
                    p.op("te", "matmul", ACC[f6][:],
                         wT[:, fi * 1536 + fo * 128: fi * 1536 + (fo + 1) * 128],
                         pb[:, fi * 512:(fi + 1) * 512],
                         start=(fi == 0), stop=(fi == 31),
                         sem=("mm" if fi == 31 else None))
            mmg = p.cnt("mm")
            p.wait("sc", "mm", mmg)
            for f6 in range(6):
                fo = grp * 6 + f6
                h, j = fo // 3, fo % 3
                p.wait("sc", "qd", p.cnt("qd"))
                p.wait("sc", "w_tr", p.cnt("w_tr"))
                p.op("sc", "activation", sstg[:], ACC[f6][:], AF.Silu,
                     sem="ex")
                if j == 0 or j == 1:
                    dst = qT_d if j == 0 else kT_d
                    p.wait("sy", "ex", p.cnt("ex"))
                    p.dma("sy", dst[h * 128:(h + 1) * 128, tb * 512:(tb + 1) * 512],
                          sstg[:], "qd")
                else:
                    p.wait("te", "ex", p.cnt("ex"))
                    p.wait("te", "ev", p.cnt("ev"))
                    for s4 in range(4):
                        p.op("te", "transpose", TB[:, s4 * 128:(s4 + 1) * 128],
                             sstg[:, s4 * 128:(s4 + 1) * 128], eye[:],
                             sem=("w_tr" if s4 == 3 else None))
                    p.wait("ve", "w_tr", p.cnt("w_tr"))
                    p.wait("ve", "qd", p.cnt("qd"))
                    p.op("ve", "tensor_copy", vstg[:], TB[:], sem="ev")
                    p.wait("sy", "ev", p.cnt("ev"))
                    p.dma("sy",
                          v_d[h, tb * 4:(tb + 1) * 4].rearrange("t p d -> p t d"),
                          vstg[:].rearrange("p (t d) -> p t d", d=128), "qd")
        mm_p[tb] = p.cnt("mm")
    proj_done = p.cnt("qd")

    p.wait("sy", "qd", proj_done)
    p.wait("te", "qd", proj_done)
    p.wait("ve", "qd", proj_done)
    p.wait("sc", "qd", proj_done)
    exd = {0: 0, 1: 0}
    pid = {0: 0, 1: 0}
    pdd = {0: 0, 1: 0}
    tbk = 0
    git = 0
    kv_mm = {}
    for h in range(H_C):
        for b in range(B):
            for qg in range(8):
                nkt = 4 * (qg + 1)
                qw = qwin[git % 2]
                if git >= 2:
                    p.wait("sy", "mm", kv_mm[git - 2])
                cq = p.dma("sy", qw[:],
                           qT_d[h * 128:(h + 1) * 128,
                                b * S + qg * 512: b * S + (qg + 1) * 512], "ld")
                if git >= 1:
                    p.wait("sy", "mm", kv_mm[git - 1])
                ck = p.dma("sy", kwin[:, 0:nkt * 128],
                           kT_d[h * 128:(h + 1) * 128, b * S: b * S + nkt * 128],
                           "ld")
                cv = p.dma("sy", vwin[:].rearrange("p (t d) -> p t d", d=128)[:, 0:nkt, :],
                           v_d[h, b * 32: b * 32 + nkt].rearrange("t p d -> p t d"),
                           "ld")
                p.wait("te", "ld", cv)
                dstart = {0: False, 1: False, 2: False, 3: False}
                iastart = {0: False, 1: False, 2: False, 3: False}
                for kt in range(nkt):
                    sbk = kt % 2
                    sp = SC[sbk]
                    I = kt // 2
                    if kt < 4 * qg:
                        c0_, c1_ = 0, 512
                    elif kt < 4 * qg + 2:
                        c0_, c1_ = 0, 512
                    else:
                        c0_, c1_ = 256, 512
                    p.wait("te", "ex", exd[sbk])
                    p.op("te", "matmul", sp[:, c0_:c1_],
                         kwin[:, kt * 128:(kt + 1) * 128],
                         qw[:, c0_:c1_], start=True, stop=True, sem="mm")
                    mm_now = p.cnt("mm")
                    if kt < 4 * qg + 2:
                        p.wait("sc", "mm", mm_now)
                        p.wait("sc", "pv", pid[sbk])
                        if kt < 4 * qg:
                            e0 = 2 * qg - 1 - I
                            e1 = 2 * qg - I
                            p.op("sc", "activation", pTi[sbk][:, 0:256],
                                 sp[:, 0:256], AF.Copy,
                                 scale=kdbd[:, (h * 2 + kt % 2) * 16 + e0:
                                            (h * 2 + kt % 2) * 16 + e0 + 1])
                            p.op("sc", "activation", pTi[sbk][:, 256:512],
                                 sp[:, 256:512], AF.Copy,
                                 scale=kdbd[:, (h * 2 + kt % 2) * 16 + e1:
                                            (h * 2 + kt % 2) * 16 + e1 + 1],
                                 sem="ex")
                        else:
                            p.op("sc", "activation", pTi[sbk][:, 256:512],
                                 sp[:, 256:512], AF.Copy,
                                 scale=kdbd[:, (h * 2 + kt % 2) * 16 + 0:
                                            (h * 2 + kt % 2) * 16 + 1],
                                 sem="ex")
                        exd[sbk] = p.cnt("ex")
                        sus = range(4) if kt < 4 * qg else (2, 3)
                        p.wait("te", "ex", exd[sbk])
                        for su in sus:
                            last = (kt == 4 * qg - 1) if su < 2 else (kt == 4 * qg + 1)
                            p.op("te", "matmul", IN[su][:, 0:128],
                                 pTi[sbk][:, su * 128:(su + 1) * 128],
                                 vwin[:, kt * 128:(kt + 1) * 128],
                                 start=(not dstart[su]), stop=last,
                                 sem=("pv" if su == list(sus)[-1] else None))
                            dstart[su] = True
                        pid[sbk] = p.cnt("pv")
                    if kt >= 4 * qg:
                        dcol = 0 if kt < 4 * qg + 2 else 256
                        mhalf = kt % 2
                        p.wait("ve", "mm", mm_now)
                        p.wait("ve", "ex", exd[sbk])
                        p.wait("ve", "pv", pdd[sbk])
                        p.op("ve", "tensor_mul", pTd[sbk][:, dcol:dcol + 256],
                             sp[:, dcol:dcol + 256],
                             DT[:, (h * 2 + mhalf) * 256:(h * 2 + mhalf + 1) * 256],
                             sem="ex")
                        exd[sbk] = p.cnt("ex")
                        if mhalf == 1:
                            blk = 0 if kt < 4 * qg + 2 else 1
                            p.wait("te", "ex", exd[0])
                            p.wait("te", "ex", exd[1])
                            for sloc in range(2):
                                su = blk * 2 + sloc
                                scol = dcol + sloc * 128
                                p.op("te", "matmul", IA[:, su * 128:(su + 1) * 128],
                                     pTd[(kt - 1) % 2][:, scol:scol + 128],
                                     vwin[:, (kt - 1) * 128:kt * 128],
                                     start=True, stop=False)
                                p.op("te", "matmul", IA[:, su * 128:(su + 1) * 128],
                                     pTd[kt % 2][:, scol:scol + 128],
                                     vwin[:, kt * 128:(kt + 1) * 128],
                                     start=False, stop=True,
                                     sem=("pv" if sloc == 1 else None))
                            pdd[0] = p.cnt("pv")
                            pdd[1] = p.cnt("pv")
                kv_mm[git] = p.cnt("mm")
                p.wait("ve", "pv", p.cnt("pv"))
                p.wait("ve", "t2", p.cnt("t2"))
                p.op("ve", "tensor_copy", isb[:], IA[:], sem="t1")
                p.wait("ve", "w_tr", p.cnt("w_tr"))
                p.wait("ve", "st", p.cnt("st"))
                for su in range(4):
                    stg = astg[su % 2]
                    if su >= 2:
                        p.wait("ve", "w_tr", p.cnt("w_tr"))
                    if qg == 0 and su < 2:
                        p.op("ve", "tensor_copy", stg[:],
                             isb[:, su * 128:(su + 1) * 128], sem="t2")
                    else:
                        p.op("ve", "scalar_tensor_tensor", stg[:],
                             IN[su][:, 0:128], qdc[:, h * 2 + su % 2: h * 2 + su % 2 + 1],
                             isb[:, su * 128:(su + 1) * 128],
                             ALU.mult, ALU.add, sem="t2")
                    p.wait("te", "t2", p.cnt("t2"))
                    p.wait("te", "ev", tbk)
                    p.op("te", "transpose", TB[:, 0:128], stg[:], eye[:],
                         sem="w_tr")
                    p.wait("ve", "w_tr", p.cnt("w_tr"))
                    p.op("ve", "tensor_copy", ostg[:, su * 128:(su + 1) * 128],
                         TB[:, 0:128], sem="ev")
                    tbk = p.cnt("ev")
                p.wait("te", "t2", p.cnt("t2"))
                p.wait("sy", "ev", p.cnt("ev"))
                p.dma("sy", at_o[h * 128:(h + 1) * 128,
                             b * S + qg * 512: b * S + qg * 512 + 512],
                      ostg[:], "st")
                git += 1
    p.wait("sy", "st", p.cnt("st"))
    p.build()
    stack.close()
    return nc


def build_launch5(dbg=False):
    nc = bass.Bass("TRN2", target_bir_lowering=False, debug=False)
    stack = ExitStack()
    p = Plan(nc, stack)

    a1_d = nc.dram_tensor("attn1Tm", [D, T_C], BF16, kind="ExternalInput").ap()
    x2_d = nc.dram_tensor("x2", [T_C, D], F32, kind="ExternalInput").ap()
    x2T_d = nc.dram_tensor("x2T", [D, T_C], BF16, kind="ExternalInput").ap()
    gw_d = nc.dram_tensor("gatew", [D, D], F32, kind="ExternalInput").ap()
    ow_d = nc.dram_tensor("outw", [D, D], F32, kind="ExternalInput").ap()
    fw_d = nc.dram_tensor("ffnw", [D, D], F32, kind="ExternalInput").ap()
    fb_d = nc.dram_tensor("ffnb", [D, 1], F32, kind="ExternalInput").ap()
    nw_d = nc.dram_tensor("normwb", [128, D], F32, kind="ExternalInput").ap()
    lA_d = nc.dram_tensor("lA", [3, 16, D], F32, kind="ExternalInput").ap()
    lB_d = nc.dram_tensor("lB", [3, D, 16], F32, kind="ExternalInput").ap()
    lsi_d = nc.dram_tensor("lsi", [1, 4], I32, kind="ExternalInput").ap()
    bsel_d = nc.dram_tensor("bsel", [1, 1], F32, kind="ExternalInput").ap()
    eye_d = nc.dram_tensor("eye", [128, 128], F32, kind="ExternalInput").ap()
    x3_d = nc.dram_tensor("x3d", [T_C, D], F32,
                          kind=("ExternalOutput" if dbg else "Internal")).ap()
    out_o = nc.dram_tensor("outp", [T_C, D], F32, kind="ExternalOutput").ap()
    if dbg:
        yb_o = nc.dram_tensor("ybo", [128, 8 * D], BF16, kind="ExternalOutput").ap()
        gs_o = nc.dram_tensor("gso", [1, 64], F32, kind="ExternalOutput").ap()
        xa_o = nc.dram_tensor("xao", [16, 3 * 1024], BF16, kind="ExternalOutput").ap()

    sb = lambda n, s, t: stack.enter_context(nc.sbuf_tensor(n, s, t))
    ps = lambda n, s, t: stack.enter_context(nc.psum_tensor(n, s, t))

    eye = sb("eye_sb", [128, 128], BF16)
    bufA = sb("bufA", [128, 32 * T_C], BF16)
    bufB = sb("bufB", [128, 8 * D], BF16)
    nat_ = sb("nat0", [128, D], BF16)
    nat = [nat_, nat_]
    wTs_ = sb("wTs0", [128, 32 * 128], BF16)
    wTs = [wTs_, wTs_]
    ystg = sb("ystg", [128, 1024], BF16)
    rstg = sb("rstg", [128, D], F32)
    astg = rstg
    bstg = sb("bstg", [128, D], BF16)
    junk = bstg
    atile = sb("atile", [128, D], BF16)
    nwb = sb("nwb", [128, D], BF16)
    fbc = sb("fbc", [128, 32 * 16], F32)
    lAT = sb("lAT", [128, 3 * 32 * 16], BF16)
    lBn = sb("lBn", [128, 3 * 32 * 16], BF16)
    lbt = sb("lbt", [16, 128], BF16)
    xa = sb("xa", [16, 3 * 1024], BF16)
    lsif = sb("lsif", [1, 64], F32)
    ones16 = sb("ones16", [1, 16], BF16)
    gbf = sb("gbf", [1, 16], BF16)
    g16 = sb("g16", [16, 3 * 16], F32)
    gsl = sb("gsl", [1, 64], F32)
    stt = sb("stt", [128, 16 * 16], F32)
    trp = [ps(f"trp{i}", [128, 512], BF16) for i in range(2)]
    ACC = [ps(f"ACC{i}", [128, 512], F32) for i in range(2)]
    XA = ps("XA", [16, 1024], F32)

    for s_ in ("ld", "w_ld", "w_tr", "w_ev", "mm", "ev", "st",
               "t1", "t2", "t3", "e1", "e2"):
        p.sem(s_)
    hist = {"n": 0, "ev": {}}

    p.op("ve", "memset", stt[:, 160:161], 1e-5)
    p.op("ve", "memset", ones16[:], 1.0)
    p.dma("gp", eye[:], eye_d[:], "ld")
    p.dma("gp", nwb[:], nw_d[:], "ld")
    p.dma("sy", fbc[:].rearrange("p (a s) -> p a s", s=16)[:, :, 0:1],
          fb_d.rearrange("(a p) b -> p a b", p=128), "ld",
          allow_slow_non_contiguous=True)
    p.dma("gp", lsif[0:1, 0:4].bitcast(I32), lsi_d[:], "ld")
    p.dma("sy", gsl[0:1, 60:61], bsel_d[:], "ld")
    c0 = p.dma("sy", bufA[:].rearrange("p (f t) -> p f t", t=T_C),
               x2T_d.rearrange("(f p) t -> p f t", p=128), "ld")

    p.wait("ve", "ld", p.cnt("ld"))
    p.op("ve", "tensor_copy", lsif[0:1, 32:36], lsif[0:1, 0:4].bitcast(I32),
         sem="t1")
    p.wait("gp", "t1", p.cnt("t1"))
    p.op("gp", "tensor_scalar", lsif[0:1, 16:20], lsif[0:1, 32:36],
         gsl[0:1, 60:61], None, ALU.is_le, sem="t2")
    c_ = lambda j: lsif[0:1, 16 + j:17 + j]
    p.wait("ve", "t2", p.cnt("t2"))
    p.op("ve", "tensor_single_scalar", gsl[0:1, 44:45], c_(3), 3.0, ALU.add,
         sem="t1")
    p.wait("gp", "t1", p.cnt("t1"))
    p.op("gp", "tensor_tensor", gsl[0:1, 46:47], gsl[0:1, 44:45], c_(2),
         ALU.mult, sem="t2")
    p.wait("ve", "t2", p.cnt("t2"))
    p.op("ve", "tensor_tensor", gsl[0:1, 50:51], c_(1), c_(0), ALU.mult,
         sem="t1")
    p.wait("gp", "t1", p.cnt("t1"))
    p.op("gp", "tensor_scalar", gsl[0:1, 52:53], c_(1), 2.0, c_(0),
         ALU.mult, ALU.add, sem="t2")
    p.wait("ve", "t2", p.cnt("t2"))
    p.op("ve", "tensor_sub", gsl[0:1, 54:55], gsl[0:1, 52:53],
         gsl[0:1, 50:51], sem="t1")
    p.wait("gp", "t1", p.cnt("t1"))
    p.op("gp", "tensor_scalar", gsl[0:1, 56:57], c_(2), -1.0, 1.0,
         ALU.mult, ALU.add, sem="t2")
    p.wait("ve", "t2", p.cnt("t2"))
    p.op("ve", "tensor_mul", gsl[0:1, 58:59], gsl[0:1, 54:55],
         gsl[0:1, 56:57], sem="t1")
    p.wait("gp", "t1", p.cnt("t1"))
    p.op("gp", "tensor_add", gsl[0:1, 48:49], gsl[0:1, 46:47],
         gsl[0:1, 58:59], sem="t2")
    p.wait("ve", "t2", p.cnt("t2"))
    p.op("ve", "tensor_single_scalar", gsl[0:1, 0:1], gsl[0:1, 48:49],
         2.0, ALU.is_equal, sem="t1")
    p.wait("gp", "t1", p.cnt("t1"))
    p.op("gp", "tensor_single_scalar", gsl[0:1, 16:17], gsl[0:1, 48:49],
         3.0, ALU.is_equal, sem="t2")
    p.wait("ve", "t2", p.cnt("t2"))
    p.op("ve", "tensor_single_scalar", gsl[0:1, 32:33], gsl[0:1, 48:49],
         3.5, ALU.is_gt, sem="t3")
    for j in range(3):
        p.wait("gp", "t3", p.cnt("t3"))
        p.op("gp", "tensor_copy", gbf[0:1, j:j + 1], gsl[0:1, 16 * j:16 * j + 1],
             sem="t1")
        p.wait("te", "t1", p.cnt("t1"))
        p.op("te", "matmul", ACC[0][0:16, j:j + 1], ones16[:],
             gbf[0:1, j:j + 1], start=True, stop=True, sem="t2")
        p.wait("ve", "t2", p.cnt("t2"))
        p.op("ve", "tensor_copy", g16[:, 16 * j:16 * j + 1],
             ACC[0][0:16, j:j + 1], sem="t3")
    g_done = p.cnt("t3")

    def ln_tile():
        p.op("sc", "activation", junk[:], rstg[:], AF.Copy,
             accum_out=stt[:, 0:1])
        p.op("sc", "activation", junk[:], rstg[:], AF.Square,
             accum_out=stt[:, 16:17], sem="e2")
        p.wait("ve", "e2", p.cnt("e2"))
        p.op("ve", "tensor_single_scalar", stt[:, 32:33], stt[:, 0:1],
             1.0 / D, ALU.mult, sem="t1")
        p.wait("sc", "t1", p.cnt("t1"))
        p.op("sc", "activation", stt[:, 48:49], stt[:, 32:33], AF.Square,
             sem="e2")
        p.wait("ve", "e2", p.cnt("e2"))
        p.op("ve", "scalar_tensor_tensor", stt[:, 64:65], stt[:, 16:17],
             1.0 / D, stt[:, 48:49], ALU.mult, ALU.subtract, sem="t1")
        p.wait("sc", "t1", p.cnt("t1"))
        p.op("sc", "activation", stt[:, 80:81], stt[:, 64:65], AF.Sqrt,
             bias=stt[:, 160:161], sem="e2")
        p.wait("ve", "e2", p.cnt("e2"))
        p.op("ve", "reciprocal", stt[:, 96:97], stt[:, 80:81], sem="t1")
        p.wait("gp", "t1", p.cnt("t1"))
        p.op("gp", "tensor_scalar", astg[:], rstg[:], stt[:, 32:33],
             stt[:, 96:97], ALU.subtract, ALU.mult, sem="t2")
        p.wait("sc", "t2", p.cnt("t2"))
        p.op("sc", "copy", bstg[:], astg[:], sem="e2")

    def alpha_to_tok(w_d, act=None):
        mmf = {}
        for fo in range(32):
            p.wait("gp", "w_tr", p.cnt("w_tr"))
            cw = p.dma("gp", nat[0][:], w_d[fo * 128:(fo + 1) * 128, :], "w_ld")
            p.wait("te", "w_ld", cw)
            if fo >= 1:
                p.wait("te", "mm", mmf[fo - 1])
            transpose_in(p, nat[0],
                         lambda j: wTs[0][:, j * 128:(j + 1) * 128],
                         32, trp, eye[:], "w_tr", "w_ev", hist)
            p.wait("te", "w_ev", p.cnt("w_ev"))
            p.wait("te", "e1", p.cnt("e1"))
            for fi in range(32):
                p.op("te", "matmul", ACC[fi % 0 if False else 0][:] if False else ACC[0][:],
                     wTs[0][:, fi * 128:(fi + 1) * 128],
                     bufA[:, fi * T_C + 0 * 512: fi * T_C + 512],
                     start=(fi == 0), stop=False, sem=None) if False else None
            for ch in range(2):
                for fi in range(32):
                    p.op("te", "matmul", ACC[ch][:],
                         wTs[0][:, fi * 128:(fi + 1) * 128],
                         bufA[:, fi * T_C + ch * 512: fi * T_C + (ch + 1) * 512],
                         start=(fi == 0), stop=(fi == 31),
                         sem=("mm" if fi == 31 else None))
            mmf[fo] = p.cnt("mm")
            yield fo, mmf[fo]

    p.wait("te", "ld", c0)
    for fo, mmc in alpha_to_tok(gw_d):
        p.wait("sc", "mm", mmc)
        p.wait("sc", "w_tr", p.cnt("w_tr"))
        p.op("sc", "activation", ystg[:, 0:512], ACC[0][:], AF.Sigmoid)
        p.op("sc", "activation", ystg[:, 512:1024], ACC[1][:], AF.Sigmoid,
             sem="e1")
        p.wait("te", "e1", p.cnt("e1"))
        transpose_in(p, ystg,
                     lambda j, fo=fo: bufB[:, j * D + fo * 128: j * D + (fo + 1) * 128],
                     8, trp, eye[:], "w_tr", "w_ev", hist)
    gate_done = p.cnt("w_ev")

    p.wait("sy", "mm", p.cnt("mm"))
    ca = p.dma("sy", bufA[:].rearrange("p (f t) -> p f t", t=T_C),
               a1_d.rearrange("(f p) t -> p f t", p=128), "ld")
    p.wait("te", "ld", ca)
    p.wait("te", "w_ev", gate_done)
    p.wait("ve", "ld", p.cnt("ld"))
    for tt in range(8):
        transpose_in(p, None if False else bufA,
                     lambda j, tt=tt: atile[:, j * 128:(j + 1) * 128],
                     32, trp, eye[:], "w_tr", "w_ev", hist,
                     ) if False else None
        for g in range(0, 32, 4):
            idx = hist["n"]
            tp = trp[idx % 2]
            if idx >= 2:
                p.wait("te", "w_ev", hist["ev"][idx - 2])
            for j in range(4):
                fi = g + j
                p.op("te", "transpose", tp[:, j * 128:(j + 1) * 128],
                     bufA[:, fi * T_C + tt * 128: fi * T_C + (tt + 1) * 128],
                     eye[:], sem=("w_tr" if j == 3 else None))
            p.wait("ve", "w_tr", p.cnt("w_tr"))
            p.op("ve", "tensor_copy", atile[:, g * 128:(g + 4) * 128], tp[:],
                 sem="w_ev")
            hist["ev"][idx] = p.cnt("w_ev")
            hist["n"] += 1
        p.wait("sc", "w_ev", p.cnt("w_ev"))
        p.op("sc", "activation", junk[:], atile[:], AF.Square,
             accum_out=stt[:, 16:17], sem="e2")
        p.wait("ve", "e2", p.cnt("e2"))
        p.op("ve", "tensor_single_scalar", stt[:, 32:33], stt[:, 16:17],
             1.0 / D, ALU.mult, sem="t1")
        p.wait("sc", "t1", p.cnt("t1"))
        p.op("sc", "activation", stt[:, 80:81], stt[:, 32:33], AF.Sqrt,
             bias=stt[:, 160:161], sem="e2")
        p.wait("ve", "e2", p.cnt("e2"))
        p.op("ve", "reciprocal", stt[:, 96:97], stt[:, 80:81], sem="t1")
        p.wait("gp", "t1", p.cnt("t1"))
        p.op("gp", "tensor_scalar", rstg[:], atile[:], stt[:, 96:97], None,
             ALU.mult, sem="t2")
        p.wait("ve", "t2", p.cnt("t2"))
        p.op("ve", "tensor_mul", rstg[:], rstg[:], nwb[:])
        p.op("ve", "tensor_mul", atile[:], rstg[:],
             bufB[:, tt * D:(tt + 1) * D], sem="e1")
        ge = p.cnt("e1")
        p.wait("te", "e1", ge)
        for g in range(0, 32, 4):
            idx = hist["n"]
            tp = trp[idx % 2]
            if idx >= 2:
                p.wait("te", "w_ev", hist["ev"][idx - 2])
            for j in range(4):
                fi = g + j
                p.op("te", "transpose", tp[:, j * 128:(j + 1) * 128],
                     atile[:, (g + j) * 128:(g + j + 1) * 128],
                     eye[:], sem=("w_tr" if j == 3 else None))
            p.wait("ve", "w_tr", p.cnt("w_tr"))
            for j in range(4):
                fi = g + j
                p.op("ve", "tensor_copy",
                     bufA[:, fi * T_C + tt * 128: fi * T_C + (tt + 1) * 128],
                     tp[:, j * 128:(j + 1) * 128],
                     sem=("w_ev" if j == 3 else None))
            hist["ev"][idx] = p.cnt("w_ev")
            hist["n"] += 1
        p.wait("gp", "w_tr", p.cnt("w_tr"))
        p.wait("sc", "w_tr", p.cnt("w_tr"))
    gn_done = p.cnt("w_ev")

    p.wait("te", "w_ev", gn_done)
    for fo, mmc in alpha_to_tok(ow_d):
        p.wait("ve", "mm", mmc)
        p.wait("ve", "w_tr", p.cnt("w_tr"))
        p.op("ve", "tensor_copy", ystg[:, 0:512], ACC[0][:])
        p.op("ve", "tensor_copy", ystg[:, 512:1024], ACC[1][:], sem="e1")
        p.wait("te", "e1", p.cnt("e1"))
        transpose_in(p, ystg,
                     lambda j, fo=fo: bufB[:, j * D + fo * 128: j * D + (fo + 1) * 128],
                     8, trp, eye[:], "w_tr", "w_ev", hist)
    y_done = p.cnt("w_ev")

    p.wait("ve", "w_ev", y_done)
    p.wait("sy", "w_ev", y_done)
    p.wait("sy", "e1", p.cnt("e1"))
    for tt in range(8):
        p.wait("sy", "t2", p.cnt("t2"))
        p.wait("sy", "st", p.cnt("st"))
        cx = p.dma("sy", rstg[:], x2_d[tt * 128:(tt + 1) * 128, :], "ld")
        p.wait("ve", "ld", cx)
        p.op("ve", "tensor_add", rstg[:], rstg[:],
             bufB[:, tt * D:(tt + 1) * D], sem="e1")
        p.wait("sc", "e1", p.cnt("e1"))
        ln_tile()
        p.wait("sy", "t2", p.cnt("t2"))
        p.dma("sy", x3_d[tt * 128:(tt + 1) * 128, :], astg[:], "st")
        p.wait("te", "e2", p.cnt("e2"))
        transpose_in(p, bstg,
                     lambda j, tt=tt: bufA[:, j * T_C + tt * 128: j * T_C + (tt + 1) * 128],
                     32, trp, eye[:], "w_tr", "w_ev", hist)
        p.wait("sc", "w_tr", p.cnt("w_tr"))
        p.wait("sy", "e1", p.cnt("e1"))
    x3_done = p.cnt("w_ev")

    p.wait("te", "w_ev", x3_done)
    p.wait("te", "t3", g_done)
    cw = p.dma("gp", lBn[:].rearrange("p (j f r) -> p j f r", j=3, f=32),
               lB_d.rearrange("j (f p) r -> p j f r", p=128), "w_ld")
    for j in range(3):
        cw = p.dma("gp", nat[0][0:16, :], lA_d[j], "w_ld")
        p.wait("te", "w_ld", cw)
        for g in range(32):
            idx = hist["n"]
            tp = trp[g % 2]
            if idx >= 2:
                p.wait("te", "w_ev", hist["ev"][idx - 2])
            p.op("te", "transpose", tp[:, 0:16],
                 nat[0][0:16, g * 128:(g + 1) * 128], eye[0:16, 0:16],
                 sem="w_tr")
            p.wait("ve", "w_tr", p.cnt("w_tr"))
            p.op("ve", "tensor_copy",
                 lAT[:, (j * 32 + g) * 16:(j * 32 + g + 1) * 16],
                 tp[:, 0:16], sem="w_ev")
            hist["ev"][idx] = p.cnt("w_ev")
            hist["n"] += 1
        p.wait("gp", "w_tr", p.cnt("w_tr"))
    lat_done = p.cnt("w_ev")
    p.wait("te", "w_ev", lat_done)
    for j in range(3):
        for ch in range(2):
            for fi in range(32):
                p.op("te", "matmul", XA[:, ch * 512:(ch + 1) * 512],
                     lAT[:, (j * 32 + fi) * 16:(j * 32 + fi + 1) * 16],
                     bufA[:, fi * T_C + ch * 512: fi * T_C + (ch + 1) * 512],
                     start=(fi == 0), stop=(fi == 31),
                     sem=("mm" if (fi == 31 and ch == 1) else None))
        p.wait("ve", "mm", p.cnt("mm"))
        p.op("ve", "tensor_single_scalar", xa[:, j * 1024:(j + 1) * 1024],
             XA[:], g16[:, 16 * j:16 * j + 1], ALU.mult, sem="e1")
        p.wait("te", "e1", p.cnt("e1"))
    xa_done = p.cnt("e1")

    p.wait("te", "e1", xa_done)
    for fo, mmc in alpha_to_tok(fw_d):
        for j in range(3):
            p.wait("te", "t3", p.cnt("t3"))
            p.op("te", "transpose", trp[0][0:16, 0:128],
                 lBn[:, (j * 32 + fo) * 16:(j * 32 + fo + 1) * 16],
                 eye[:], sem="w_tr")
            p.wait("ve", "w_tr", p.cnt("w_tr"))
            p.op("ve", "tensor_copy", lbt[:], trp[0][0:16, 0:128], sem="t2")
            p.wait("te", "t2", p.cnt("t2"))
            for ch in range(2):
                p.op("te", "matmul", ACC[ch][:],
                     lbt[:], xa[0:16, j * 1024 + ch * 512: j * 1024 + (ch + 1) * 512],
                     start=False, stop=(j == 2),
                     sem=("t3" if (j == 2 and ch == 1) else None))
        p.wait("gp", "t3", p.cnt("t3"))
        p.wait("gp", "w_tr", p.cnt("w_tr"))
        p.op("gp", "tensor_scalar", ystg[:, 0:512] if False else ystg[:, 0:512],
             ACC[0][:], fbc[:].rearrange("p (a s) -> p a s", s=16)[:, fo, 0:1],
             None, ALU.add) if False else None
        p.wait("ve", "t3", p.cnt("t3"))
        p.wait("ve", "w_tr", p.cnt("w_tr"))
        p.op("ve", "tensor_scalar", ystg[:, 0:512], ACC[0][:],
             fbc[:].rearrange("p (a s) -> p a s", s=16)[:, fo, 0:1], None,
             ALU.add)
        p.op("ve", "tensor_scalar", ystg[:, 512:1024], ACC[1][:],
             fbc[:].rearrange("p (a s) -> p a s", s=16)[:, fo, 0:1], None,
             ALU.add, sem="e1")
        p.wait("te", "e1", p.cnt("e1"))
        transpose_in(p, ystg,
                     lambda j2, fo=fo: bufB[:, j2 * D + fo * 128: j2 * D + (fo + 1) * 128],
                     8, trp, eye[:], "w_tr", "w_ev", hist)
    y2_done = p.cnt("w_ev")

    p.wait("ve", "w_ev", y2_done)
    p.wait("sy", "w_ev", y2_done)
    p.wait("sy", "e2", p.cnt("e2"))
    p.wait("sy", "e1", p.cnt("e1"))
    for tt in range(8):
        p.wait("sy", "t2", p.cnt("t2"))
        p.wait("sy", "st", p.cnt("st"))
        cx = p.dma("sy", rstg[:], x3_d[tt * 128:(tt + 1) * 128, :], "ld")
        p.wait("ve", "ld", cx)
        p.op("ve", "tensor_add", rstg[:], rstg[:],
             bufB[:, tt * D:(tt + 1) * D], sem="e1")
        p.wait("sc", "e1", p.cnt("e1"))
        ln_tile()
        p.wait("sy", "t2", p.cnt("t2"))
        p.dma("sy", out_o[tt * 128:(tt + 1) * 128, :], astg[:], "st")
        p.wait("sy", "e1", p.cnt("e1"))
    if dbg:
        p.dma("sy", yb_o[:], bufB[:], "st")
        p.dma("sy", gs_o[:], gsl[:], "st")
        p.dma("sy", xa_o[:], xa[:], "st")
    p.wait("sy", "st", p.cnt("st"))
    p.build()
    stack.close()
    return nc


def _tables_l0():
    inv = 1.0 / 10000.0 ** (np.arange(0, 128, 2, dtype=np.float64) / 128)
    ang = np.outer(inv, np.arange(S, dtype=np.float64))
    cosT = np.concatenate([np.cos(ang), np.cos(ang)], 0).astype(np.float32)
    sinT = np.concatenate([np.sin(ang), np.sin(ang)], 0).astype(np.float32)
    r = np.arange(128)[:, None]
    i = np.arange(512)[None, :]
    tri = np.concatenate([(i >= r + 128 * j).astype(np.float32)
                          for j in range(4)], 1)
    eye = np.eye(128, dtype=np.float32)
    return (np.ascontiguousarray(cosT[:64]), np.ascontiguousarray(cosT[64:]),
            np.ascontiguousarray(sinT[:64]), np.ascontiguousarray(sinT[64:]),
            tri, eye)


def _tables_l1(c):
    base = 1.0 / 2.0 ** (8.0 / HEADS)
    factor = 1.0 - 1 / (32 - 1 + 1e-5) + 1e-5
    pvec = np.arange(128, dtype=np.float64)
    qdc = np.zeros((128, 8))
    kdbd = np.zeros((128, 128))
    DT = np.zeros((128, 2048))
    for hl in range(H_C):
        slope = base ** (4 * c + hl + 1) * factor
        for half in range(2):
            m = half * 128 + pvec
            qdc[:, hl * 2 + half] = np.exp(-slope * (m + 1))
            bd = np.exp(np.float64(-slope * 256.0))
            kdm = np.exp(-slope * (255.0 - m))
            for e in range(16):
                kdbd[:, (hl * 2 + half) * 16 + e] = kdm * (bd ** e)
            n = np.arange(256, dtype=np.float64)[None, :]
            dif = n - m[:, None]
            DT[:, (hl * 2 + half) * 256:(hl * 2 + half + 1) * 256] = \
                np.where(dif >= 0, np.exp(-slope * dif), 0.0)
    return (qdc.astype(np.float32), kdbd.astype(np.float32),
            DT.astype(np.float32))


_progs = {}


def _prog(name, builder):
    if name not in _progs:
        _progs[name] = builder()
    return _progs[name]


def kernel(**inputs):
    g = lambda k: np.ascontiguousarray(np.asarray(inputs[k], dtype=np.float32))
    x = g("L0_attn_hidden_states").reshape(T_ALL, D)
    cosL, cosH, sinL, sinH, tri, eye = _tables_l0()
    cores = list(range(NCORES))

    nc1 = _prog("l1", build_launch1)
    in1 = [{"x": np.ascontiguousarray(x[c * T_C:(c + 1) * T_C]),
            "wkr": g("L0_attn_Wkr"), "wdk": g("L0_attn_Wdk"), "eye": eye}
           for c in cores]
    r1 = run_bass_kernel_spmd(nc1, in1, core_ids=cores).results
    xT = np.concatenate([np.asarray(r1[c]["xT"]) for c in cores], 1)
    krT = np.concatenate([np.asarray(r1[c]["krT"]) for c in cores], 1)
    ckvT = np.concatenate([np.asarray(r1[c]["ckvT"]) for c in cores], 1)

    nc2 = _prog("l2", build_launch2)
    Wq, Wupk, Wupv = g("L0_attn_Wq"), g("L0_attn_Wupk"), g("L0_attn_Wupv")
    in2 = [{"xT": xT, "krTm": np.ascontiguousarray(krT[64 * c:64 * (c + 1)]),
            "ckvT": ckvT,
            "wq": np.ascontiguousarray(Wq[512 * c:512 * (c + 1)]),
            "wupk": np.ascontiguousarray(Wupk[64 * c:64 * (c + 1)]),
            "wupv": np.ascontiguousarray(Wupv[128 * c:128 * (c + 1)]),
            "cosL": cosL, "cosH": cosH, "sinL": sinL, "sinH": sinH,
            "tri": tri, "eye": eye} for c in cores]
    r2 = run_bass_kernel_spmd(nc2, in2, core_ids=cores).results
    attnT = np.concatenate([np.asarray(r2[c]["attnT"]) for c in cores], 0)

    nc3 = _prog("l3", build_launch3)
    in3 = [{"attnTm": np.ascontiguousarray(attnT[:, c * T_C:(c + 1) * T_C]),
            "x": np.ascontiguousarray(x[c * T_C:(c + 1) * T_C]),
            "wo": g("L0_attn_Wo"), "fw": g("L0_ffn_weight"),
            "gw": g("L0_ffn_gate_w"), "ics": g("L0_ffn_in_channel_scale"),
            "ocs": g("L0_ffn_out_channel_scale"),
            "fb": np.ascontiguousarray(g("L0_ffn_bias").reshape(D, 1)),
            "eye": eye} for c in cores]
    r3 = run_bass_kernel_spmd(nc3, in3, core_ids=cores).results
    x2 = [np.asarray(r3[c]["x2"]) for c in cores]
    x2T = np.concatenate([np.asarray(r3[c]["x2T"]) for c in cores], 1)

    nc4 = _prog("l4", build_launch4)
    qkvw = g("L1_attn_qkv_w")
    in4 = []
    for c in cores:
        qdc, kdbd, DT = _tables_l1(c)
        in4.append({"x2T": x2T,
                    "qkvw": np.ascontiguousarray(qkvw[1536 * c:1536 * (c + 1)]),
                    "qdc": qdc, "kdbd": kdbd, "DTt": DT, "eye": eye})
    r4 = run_bass_kernel_spmd(nc4, in4, core_ids=cores).results
    attn1T = np.concatenate([np.asarray(r4[c]["attn1T"]) for c in cores], 0)

    nc5 = _prog("l5", build_launch5)
    nwb = np.ascontiguousarray(
        np.broadcast_to(g("L1_attn_norm_w")[None, :], (128, D)))
    lsi = np.ascontiguousarray(
        np.asarray(inputs["L1_ffn_lora_start_indices"], dtype=np.int32)
        .reshape(1, 4))
    in5 = [{"attn1Tm": np.ascontiguousarray(attn1T[:, c * T_C:(c + 1) * T_C]),
            "x2": x2[c], "x2T": np.ascontiguousarray(x2T[:, c * T_C:(c + 1) * T_C]),
            "gatew": g("L1_attn_gate_w"), "outw": g("L1_attn_out_w"),
            "ffnw": g("L1_ffn_weight"),
            "ffnb": np.ascontiguousarray(g("L1_ffn_bias").reshape(D, 1)),
            "normwb": nwb, "lA": g("L1_ffn_lora_A"), "lB": g("L1_ffn_lora_B"),
            "lsi": lsi, "bsel": np.array([[float(c // 4)]], dtype=np.float32),
            "eye": eye} for c in cores]
    r5 = run_bass_kernel_spmd(nc5, in5, core_ids=cores).results
    out = np.concatenate([np.asarray(r5[c]["outp"]) for c in cores], 0)
    return out.reshape(B, S, D)
